# version 03 (final)
# baseline (speedup 1.0000x reference)
import math
import numpy as np
import concourse.bass as bass
import concourse.mybir as mybir
from concourse.bass_utils import run_bass_kernel_spmd

F32 = mybir.dt.float32
BF16 = mybir.dt.bfloat16
AF = mybir.ActivationFunctionType
ALU = mybir.AluOpType
AX = mybir.AxisListType

NCORES = 8
P = 128
G = 8
C = 1024
T = 2048
HALO = 258
EX = T + HALO
OWN0 = HALO
BA0 = HALO - 1
NL = 6
HEADS = 16
DH = 64
WIN = 256
FFI = 2816
FFC = FFI // P
QK_SCALE = 8.0
NEG = -1.0e30


class EngQ:

    def __init__(self, ctx, name, sem, dma_sem=None):
        self.ctx = ctx
        self.name = name
        self.sem = sem
        self.count = 0
        self.dma_sem = dma_sem
        self.dma_count = 0
        self.waited = {}
        self.ops = []

    def wait(self, ms):
        if ms is None:
            return
        if isinstance(ms, list):
            for m in ms:
                self.wait(m)
            return
        sem, sname, val = ms
        if val <= 0:
            return
        if sname == self.name:
            return
        if not getattr(self, "no_elide", False) and \
                self.waited.get(sname, 0) >= val:
            return
        self.waited[sname] = max(self.waited.get(sname, 0), val)
        self.ops.append(lambda e, sem=sem, val=val: e.wait_ge(sem, val))

    def wait_all(self, mss):
        for ms in mss:
            self.wait(ms)

    def run(self, fn, mark=False):
        if mark:
            self.count += 1

            def wrapped(e, fn=fn):
                ins = fn(e)
                ins.then_inc(self.sem)
            self.ops.append(wrapped)
            return (self.sem, self.name, self.count)
        else:
            self.ops.append(fn)
            return None

    def dma(self, out_ap, in_ap, **kw):
        self.dma_count += 16

        def fn(e, out_ap=out_ap, in_ap=in_ap, kw=kw):
            e.dma_start(out_ap, in_ap, **kw).then_inc(self.dma_sem[0], 16)
        self.ops.append(fn)
        return (self.dma_sem[0], self.dma_sem[1], self.dma_count)

    def milestone(self):
        self.count += 1
        ap = getattr(self, "fence_ap", None)

        def fn(e, ap=ap):
            if ap is not None and hasattr(e, "memset"):
                e.memset(ap, 0.0).then_inc(self.sem)
            elif ap is not None and hasattr(e, "memzero"):
                e.memzero(ap).then_inc(self.sem)
            elif hasattr(e, "engine_nop"):
                e.engine_nop().then_inc(self.sem)
            else:
                e.nop().then_inc(self.sem)
        self.ops.append(fn)
        return (self.sem, self.name, self.count)

    def last(self):
        return (self.sem, self.name, self.count)

    def last_dma(self):
        return (self.dma_sem[0], self.dma_sem[1], self.dma_count)

    def self_fence(self):
        sem, val = self.sem, self.count
        self.ops.append(lambda e, sem=sem, val=val: e.wait_ge(sem, val))


class Ctx:
    def __init__(self, nc):
        self.nc = nc
        mk = nc.alloc_semaphore
        self.sem_t = mk("sem_t")
        self.sem_v = mk("sem_v")
        self.sem_s = mk("sem_s")
        self.sem_p = mk("sem_p")
        self.sem_dy = mk("sem_dy")
        self.sem_dp = mk("sem_dp")
        self.sem_cc = mk("sem_cc")
        self.cc_count = 0
        self.t = EngQ(self, "t", self.sem_t)
        self.v = EngQ(self, "v", self.sem_v)
        self.s = EngQ(self, "s", self.sem_s)
        self.p = EngQ(self, "p", self.sem_p, dma_sem=(self.sem_dp, "dp"))
        self.y = EngQ(self, "y", None, dma_sem=(self.sem_dy, "dy"))
        self.queues = [self.t, self.v, self.s, self.p, self.y]

    def barrier(self):
        marks = [self.t.last()]
        for q in (self.v, self.s, self.p):
            marks.append(q.milestone())
        marks.append(self.y.last_dma())
        marks.append(self.p.last_dma())
        for q in self.queues:
            q.wait_all(marks)

    def collective(self, kind, op, rg, ins, outs):
        self.cc_count += 1

        def fn(e, kind=kind, op=op, rg=rg, ins=ins, outs=outs):
            e.collective_compute(kind, op, replica_groups=rg, ins=ins,
                                 outs=outs).then_inc(self.sem_cc)
        self.p.ops.append(fn)
        return (self.sem_cc, "cc", self.cc_count)


def _colvecs(vec):
    return np.ascontiguousarray(vec.reshape(G, P).T).astype(np.float32)


def _rotary_tables():
    n = 8192
    half = np.arange(0, DH, 2, dtype=np.float64)
    inv_freq = 1.0 / (10000.0 ** (half / DH))
    t = np.arange(n, dtype=np.float64)
    freqs = t[:, None] * inv_freq[None, :]
    freqs = np.concatenate([freqs, freqs], -1)
    base = (half + 0.4 * DH) / (1.4 * DH)
    power = (t - n // 2) / (WIN // 2)
    scale = base[None, :] ** power[:, None]
    scale = np.concatenate([scale, scale], -1)
    cos, sin = np.cos(freqs), np.sin(freqs)
    A = cos * scale
    B = sin * scale
    Ai = cos / scale
    Bi = sin / scale
    return [x.astype(np.float32) for x in (A, B, Ai, Bi)]


def _mask_tiles():
    reg = np.zeros((2, P, 2 * WIN), np.float32)
    first = np.zeros((2, P, 2 * WIN), np.float32)
    for h in range(2):
        i = np.arange(P)[:, None]
        k = np.arange(2 * WIN)[None, :]
        d = 128 * h + i + 256 - k
        bad = (d < 0) | (d > 256)
        reg[h][bad] = NEG
        firstbad = bad | (k < WIN)
        first[h][firstbad] = NEG
    return reg, first


def host_prep(I):
    f32 = lambda a: np.ascontiguousarray(a, dtype=np.float32)
    x = f32(I["L0_attn_x"])

    shared = {}
    shared["wkT"] = f32(I["L0_attn_Wk"].T)
    shared["wvT"] = f32(I["L0_attn_Wv"].T)
    shared["wrT"] = f32(I["L0_attn_Wr"].T)
    shared["woT"] = f32(I["L0_attn_Wo"].T)
    shared["wkfT"] = f32(I["L0_attn_Wk_ffn"].T)
    shared["wvfT"] = f32(I["L0_attn_Wv_ffn"].T)
    shared["wrfT"] = f32(I["L0_attn_Wr_ffn"].T)
    for nm, key in [("ln1g", "L0_attn_ln1_g"), ("ln1b", "L0_attn_ln1_b"),
                    ("ln2g", "L0_attn_ln2_g"), ("ln2b", "L0_attn_ln2_b"),
                    ("amk", "L0_attn_att_mix_k"), ("amv", "L0_attn_att_mix_v"),
                    ("amr", "L0_attn_att_mix_r"), ("fmk", "L0_attn_ffn_mix_k"),
                    ("fmr", "L0_attn_ffn_mix_r"), ("tdec", "L0_attn_time_decay"),
                    ("tfir", "L0_attn_time_first")]:
        shared[nm] = _colvecs(f32(I[key]).reshape(-1))

    shared["wqkv"] = f32(I["L0_ffn_Wqkv"]).reshape(NL * C, 3 * C)
    shared["wo_b"] = f32(I["L0_ffn_Wo"]).reshape(NL * C, C)
    W1 = f32(I["L0_ffn_W1"])
    DI = 2730
    w1a = np.zeros((NL, C, FFI), np.float32)
    w1g = np.zeros((NL, C, FFI), np.float32)
    w1a[:, :, :DI] = W1[:, :, :DI]
    w1g[:, :, :DI] = W1[:, :, DI:]
    shared["w1a"] = w1a.reshape(NL * C, FFI)
    shared["w1g"] = w1g.reshape(NL * C, FFI)
    W2 = f32(I["L0_ffn_W2"])
    w2p = np.zeros((NL, FFI, C), np.float32)
    w2p[:, :DI, :] = W2
    shared["w2p"] = w2p.reshape(NL * FFI, C)
    b1 = f32(I["L0_ffn_b1"])
    b1a = np.zeros((NL, FFI), np.float32)
    b1g = np.zeros((NL, FFI), np.float32)
    b1a[:, :DI] = b1[:, :DI]
    b1g[:, :DI] = b1[:, DI:]
    shared["b1a"] = np.ascontiguousarray(
        b1a.reshape(NL, FFC, P).transpose(0, 2, 1)).reshape(NL * P, FFC)
    shared["b1g"] = np.ascontiguousarray(
        b1g.reshape(NL, FFC, P).transpose(0, 2, 1)).reshape(NL * P, FFC)
    shared["b2v"] = np.concatenate(
        [_colvecs(f32(I["L0_ffn_b2"])[l]) for l in range(NL)])
    shared["lngv"] = np.concatenate(
        [_colvecs(f32(I["L0_ffn_ln_g"])[l]) for l in range(NL)])
    shared["lnbv"] = np.concatenate(
        [_colvecs(f32(I["L0_ffn_ln_b"])[l]) for l in range(NL)])
    qs = f32(I["L0_ffn_q_scale"])
    ks = f32(I["L0_ffn_k_scale"])
    shared["qsv"] = np.tile(qs, (1, 2)).reshape(NL * P, 1)
    shared["ksv"] = np.tile(ks, (1, 2)).reshape(NL * P, 1)

    ident = np.eye(P, dtype=np.float32)
    shared["ident"] = ident
    shared["onesr"] = np.ones((1, P), np.float32)
    shared["onesc"] = np.ones((P, 1), np.float32)
    sel64 = np.zeros((P, 2), np.float32)
    sel64[:DH, 0] = 1.0
    sel64[DH:, 1] = 1.0
    shared["sel64"] = sel64
    shared["selT2"] = np.ascontiguousarray(sel64.T)
    iota = np.tile(np.arange(T, dtype=np.float32)[None, :], (P, 1))
    shared["iota"] = np.ascontiguousarray(iota)

    Atab, Btab, Aitab, Bitab = _rotary_tables()
    reg, first = _mask_tiles()

    in_maps = []
    for q in range(NCORES):
        b, j = q // 4, q % 4
        s = T * j
        m = dict(shared)
        m["x_own"] = f32(x[b, s:s + T])
        if j > 0:
            m["x_halo"] = f32(x[b, s - HALO:s])
        else:
            m["x_halo"] = np.zeros((HALO, C), np.float32)
        pos = np.arange(s - HALO, s + T)
        pos = np.clip(pos, 0, 8191)
        def duptab(tab):
            sl = tab[pos]
            return np.ascontiguousarray(
                np.concatenate([sl, sl], axis=1).T)
        m["atab"] = duptab(Atab)
        m["btab"] = duptab(Btab)
        m["aitab"] = duptab(Aitab)
        m["bitab"] = duptab(Bitab)
        m["maskreg"] = np.ascontiguousarray(
            np.concatenate([reg[0], reg[1]], axis=1))
        m["maskw0"] = np.ascontiguousarray(
            np.concatenate([(first if j == 0 else reg)[0],
                            (first if j == 0 else reg)[1]], axis=1))
        sel = np.zeros((P, NCORES), np.float32)
        if j > 0:
            sel[:, q - 1] = 1.0
        m["selv"] = sel
        stA = np.full(NCORES, 1e9, np.float32)
        stH1 = np.full(NCORES, 1e9, np.float32)
        stH2 = np.full(NCORES, 1e9, np.float32)
        for r in range(NCORES):
            br, jr = r // 4, r % 4
            if br == b and jr < j:
                stA[r] = float(T * (j - 1 - jr))
                if jr <= j - 2:
                    stH1[r] = float(T * (j - 1 - jr) - 1)
                if jr == j - 1:
                    stH2[r] = 0.0
        m["stepsA"] = np.tile(stA[None, :], (P, 1))
        m["stepsH1"] = np.tile(stH1[None, :], (P, 1))
        m["stepsH2"] = np.tile(stH2[None, :], (P, 1))
        m["hasleft"] = np.full((P, 1), 1.0 if j > 0 else 0.0, np.float32)
        in_maps.append(m)
    return in_maps


IN_SPECS = [
    ("x_own", [T, C]), ("x_halo", [HALO, C]),
    ("wkT", [C, C]), ("wvT", [C, C]), ("wrT", [C, C]), ("woT", [C, C]),
    ("wkfT", [C, 4096]), ("wvfT", [4096, C]), ("wrfT", [C, C]),
    ("ln1g", [P, G]), ("ln1b", [P, G]), ("ln2g", [P, G]), ("ln2b", [P, G]),
    ("amk", [P, G]), ("amv", [P, G]), ("amr", [P, G]),
    ("fmk", [P, G]), ("fmr", [P, G]), ("tdec", [P, G]), ("tfir", [P, G]),
    ("wqkv", [NL * C, 3 * C]), ("wo_b", [NL * C, C]),
    ("w1a", [NL * C, FFI]), ("w1g", [NL * C, FFI]), ("w2p", [NL * FFI, C]),
    ("b1a", [NL * P, FFC]), ("b1g", [NL * P, FFC]), ("b2v", [NL * P, G]),
    ("lngv", [NL * P, G]), ("lnbv", [NL * P, G]),
    ("qsv", [NL * P, 1]), ("ksv", [NL * P, 1]),
    ("ident", [P, P]), ("onesr", [1, P]), ("onesc", [P, 1]),
    ("sel64", [P, 2]), ("selT2", [2, P]), ("iota", [P, T]),
    ("atab", [P, EX]), ("btab", [P, EX]), ("aitab", [P, EX]), ("bitab", [P, EX]),
    ("maskreg", [P, 4 * WIN]), ("maskw0", [P, 4 * WIN]),
    ("selv", [P, NCORES]), ("stepsA", [P, NCORES]), ("stepsH1", [P, NCORES]),
    ("stepsH2", [P, NCORES]), ("hasleft", [P, 1]),
]


def build_program(stage="full", debug_taps=(), nl_build=NL, dump_xs=False):
    nc = bass.Bass("TRN2", target_bir_lowering=False, debug=False,
                   dynamic_dma_scratch_size=16384)
    D = {}
    for name, shape in IN_SPECS:
        D[name] = nc.dram_tensor(name, shape, F32, kind="ExternalInput").ap()
    out = nc.dram_tensor("out", [T, C], F32, kind="ExternalOutput").ap()
    dbg = {}
    for name, shape in debug_taps:
        dbg[name] = nc.dram_tensor(name, shape, F32, kind="ExternalOutput").ap()

    espill = nc.dram_tensor("espill", [C, T], F32).ap()
    vspill = nc.dram_tensor("vspill", [C, T], F32).ap()
    srspill = nc.dram_tensor("srspill", [C, T], F32).ap()
    sspill = nc.dram_tensor("sspill", [C, T], F32).ap()
    yspill = nc.dram_tensor("yspill", [C, T + 8], F32).ap()
    xaspill = nc.dram_tensor("xaspill", [C, T + 8], F32).ap()
    gwkv_in = nc.dram_tensor("gwkv_in", [P, 32], F32)
    gwkv_out = nc.dram_tensor("gwkv_out", [NCORES * P, 32], F32)
    gbb_in = nc.dram_tensor("gbb_in", [C, HALO], F32)
    gbb_out = nc.dram_tensor("gbb_out", [NCORES * C, HALO], F32)

    A = nc.alloc_sbuf_tensor
    XS = A("XS", [P, G * EX], F32)
    PBF = A("PBF", [P, 24576], BF16)
    PBF2 = A("PBF2", [P, 15424], BF16)
    PF1 = A("PF1", [P, 7508], F32)
    PF2 = A("PF2", [P, 2080], F32)
    ESTG = A("ESTG", [P, 512], F32)
    VSTG = A("VSTG", [P, 512], F32)
    SSTG = A("SSTG", [P, 512], F32)
    ROWS = A("ROWS", [2, 2080], F32)
    VEC = A("VEC", [P, 184], F32)
    IDENT = A("IDENT", [P, P], F32)
    IDENTB = A("IDENTB", [P, P], BF16)
    ONESBF = A("ONESBF", [1, P], BF16)
    ONECF = A("ONECF", [P, 1], F32)
    SEL64F = A("SEL64F", [P, 2], F32)
    SELT2B = A("SELT2B", [2, P], BF16)
    GW = A("GWT", [P, NCORES * 32], F32)
    EPSC = A("EPSC", [P, 1], F32)
    HLB = A("HLB", [P, 1024], BF16)
    ONECB = A("ONECB", [P, 1], BF16)
    SMALL = A("SMALLQ", [P, 188], F32)

    PS = [nc.alloc_psum_tensor(f"ps{i}", [P, 512], F32) for i in range(6)]
    PSROW = nc.alloc_psum_tensor("psrow", [2, 512], F32)
    PSTB = nc.alloc_psum_tensor("pstb", [P, 1024], BF16)

    import os
    cx = Ctx(nc)
    tq, vq, sq, pq, yq = cx.t, cx.v, cx.s, cx.p, cx.y
    if os.environ.get("KERN_NO_ELIDE"):
        for _q in cx.queues:
            _q.no_elide = True
    vq.fence_ap = SMALL[:, 184:185]
    sq.fence_ap = SMALL[:, 185:186]
    pq.fence_ap = SMALL[:, 186:187]

    def vcol(i):
        return VEC[:, i:i + 1]

    def vcols(i, n):
        return VEC[:, i:i + n]

    def _freesz(ap_):
        sh = ap_.shape
        n = 1
        for d in sh[1:]:
            n *= d
        return n

    def _emit(q, fn, outap, mark):
        small = _freesz(outap) < 512
        m = q.run(fn, mark=(mark or small))
        if small:
            q.self_fence()
        return m

    def ts(q, outap, inap, s1, s2, op0, op1, mark=False, accum=None):
        if s2 is None and accum is None:
            return _emit(q, lambda e: e.tensor_single_scalar(outap, inap, s1,
                                                             op0), outap, mark)
        return _emit(q, lambda e: e.tensor_scalar(outap, inap, s1, s2, op0,
                                                  op1, accum_out=accum),
                     outap if accum is None else accum, mark)

    def tt(q, outap, a, b, op, mark=False):
        return _emit(q, lambda e: e.tensor_tensor(outap, a, b, op), outap,
                     mark)

    def stt(q, outap, in0, scal, in1, op0, op1, mark=False, accum=None):
        return _emit(q, lambda e: e.scalar_tensor_tensor(outap, in0, scal,
                                                         in1, op0, op1,
                                                         accum_out=accum),
                     outap if accum is None else accum, mark)

    def act(q, outap, inap, fn, bias=0.0, scale=1.0, accum=None, mark=False):
        return _emit(q, lambda e: e.activation(outap, inap, fn, bias=bias,
                                               scale=scale, accum_out=accum),
                     outap if accum is None else accum, mark)

    def cpy(q, outap, inap, mark=False):
        return _emit(q, lambda e: e.tensor_copy(outap, inap), outap, mark)

    def mm(outap, lhsT, rhs, start, stop, mark=False):
        return tq.run(lambda e: e.matmul(outap, lhsT, rhs, start=start,
                                         stop=stop, skip_group_check=True),
                      mark=mark)

    def redu(q, outap, inap, op, negate=False, mark=False):
        return _emit(q, lambda e: e.tensor_reduce(outap, inap, AX.X, op,
                                                  negate=negate), outap, mark)

    def scan(q, outap, d0, d1, init, mark=False):
        m = q.run(lambda e: e.tensor_tensor_scan(outap, d0, d1, init,
                                                 ALU.mult, ALU.add),
                  mark=True)
        q.self_fence()
        return m

    def memset(q, ap_, val, mark=False):
        return _emit(q, lambda e: e.memset(ap_, val), ap_, mark)

    def xsg(g, c0, w):
        return XS[:, g * EX + c0: g * EX + c0 + w]

    m_const = []
    for name, dst in [("ident", IDENT[:]),
                      ("onesc", ONECF[:]), ("sel64", SEL64F[:])]:
        m_const.append(yq.dma(dst, D[name]))
    m_selt = pq.dma(SELT2B[:], D["selT2"])
    m_onesb = pq.dma(ONESBF[:], D["onesr"])
    m_identb = pq.dma(IDENTB[:], D["ident"])
    for name in ["tdec", "tfir", "amk", "amv", "amr", "fmk", "fmr"]:
        m_const.append(yq.dma(SMALL[:, {"tdec": 8, "tfir": 16, "amk": 24,
                                        "amv": 32, "amr": 40, "fmk": 48,
                                        "fmr": 56}[name]:
                              {"tdec": 8, "tfir": 16, "amk": 24, "amv": 32,
                               "amr": 40, "fmk": 48, "fmr": 56}[name] + 8],
                              D[name]))
    m_const.append(yq.dma(SMALL[:, 64:65], D["hasleft"]))
    m_const.append(yq.dma(SMALL[:, 72:80], D["selv"]))
    memset(vq, EPSC[:], 1e-5, mark=True)
    vq.wait_all(m_const)
    cpy(vq, ONECB[:], ONECF[:])
    sq.wait(vq.last())
    sq.wait_all(m_const)
    act(sq, vcols(0, 8), SMALL[:, 8:16], AF.Exp, scale=1.0)
    m_d0 = sq.run(lambda e: e.mul(vcols(0, 8), vcols(0, 8), -1.0),
                  mark=True)
    sq.self_fence()
    m_lam = act(sq, vcols(8, 8), vcols(0, 8), AF.Exp, mark=True)
    m_euc = act(sq, vcols(16, 8), SMALL[:, 16:24], AF.Exp, mark=True)
    vq.wait_all([m_d0])
    for i, src in [(24, 24), (32, 32), (40, 40), (48, 48), (56, 56)]:
        ts(vq, vcols(i, 8), SMALL[:, src:src + 8], -1.0, 1.0, ALU.mult,
           ALU.add)
    m_om = vq.milestone()

    XTM = PF1[:, 0:1024]
    XNT = PF1[:, 1024:2048]
    SQD = PF1[:, 2048:3072]
    NT = (EX + P - 1) // P
    tq.wait_all([m_const[0]])
    m_xnt_prev = None
    m_tr_prev = None
    for k in range(NT):
        r0 = k * P
        rows = min(P, EX - r0)
        yq.wait(m_xnt_prev)
        if rows < P:
            m_ms = memset(vq, XTM, 0.0, mark=True)
            yq.wait(m_ms)
        mlist = []
        h_end = min(rows, max(0, HALO - r0))
        if h_end > 0:
            mlist.append(yq.dma(XTM[0:h_end, :], D["x_halo"][r0:r0 + h_end, :]))
        if rows > h_end:
            o0 = r0 + h_end - HALO
            mlist.append(yq.dma(XTM[h_end:rows, :],
                                D["x_own"][o0:o0 + rows - h_end, :]))
        vq.wait_all(mlist)
        sq.wait_all(mlist)
        m_sx = redu(vq, SMALL[:, 80:81], XTM, ALU.add, mark=True)
        m_sq = act(sq, SQD, XTM, AF.Square, accum=SMALL[:, 81:82], mark=True)
        vq.wait(m_sq)
        ts(vq, SMALL[:, 80:81], SMALL[:, 80:81], 1.0 / C, None, ALU.mult,
           ALU.add)
        ts(vq, SMALL[:, 81:82], SMALL[:, 81:82], 1.0 / C, None, ALU.mult,
           ALU.add)
        tt(vq, SMALL[:, 82:83], SMALL[:, 80:81], SMALL[:, 80:81], ALU.mult)
        m_var = tt(vq, SMALL[:, 81:82], SMALL[:, 81:82], SMALL[:, 82:83],
                   ALU.subtract, mark=True)
        sq.wait(m_var)
        act(sq, SMALL[:, 83:84], SMALL[:, 81:82], AF.Ln, bias=EPSC[:])
        m_rstd = act(sq, SMALL[:, 83:84], SMALL[:, 83:84], AF.Exp, scale=-0.5,
                     mark=True)
        vq.wait(m_rstd)
        vq.wait(m_tr_prev)
        m_xnt = ts(vq, XNT, XTM, SMALL[:, 80:81], SMALL[:, 83:84],
                   ALU.subtract, ALU.mult, mark=True)
        m_xnt_prev = m_xnt
        tq.wait(m_xnt)
        for cc in range(G):
            bank = PS[cc % 2]
            m_tr = tq.run(lambda e, cc=cc, bank=bank:
                          e.transpose(bank[:, 0:P], XNT[:, cc * P:(cc + 1) * P],
                                      IDENT[:]), mark=True)
            m_tr_prev = m_tr
            sq.wait(m_tr)
            m_ev = act(sq, xsg(cc, r0, rows),
                       bank[:, 0:rows] if rows < P else bank[:, 0:P],
                       AF.Copy, mark=True)
            tq.wait(m_ev)
        cx.barrier()
    cx.barrier()

    def emit():
        with nc.Block() as blk:
            blk.tensor(lambda e: [op(e) for op in cx.t.ops])
            blk.vector(lambda e: [op(e) for op in cx.v.ops])
            blk.scalar(lambda e: [op(e) for op in cx.s.ops])
            blk.gpsimd(lambda e: [op(e) for op in cx.p.ops])
            blk.sync(lambda e: [op(e) for op in cx.y.ops])
        return nc

    if stage == "P0":
        dbg_xn = nc.dram_tensor("dbg_xn", [C, EX], F32,
                                kind="ExternalOutput").ap()
        for g in range(G):
            yq.dma(dbg_xn[g * P:(g + 1) * P, :], XS[:, g * EX:(g + 1) * EX])
        yq.dma(out, D["x_own"])
        yq.wait(yq.last_dma())
        return emit()

    WKT = PBF[:, 0:8192]
    WVT = PBF[:, 8192:16384]
    WRT = PBF[:, 16384:24576]
    XKB = PBF2[:, 0:4096]
    XVB = PBF2[:, 4096:8192]
    XRB = PBF2[:, 8192:12288]
    XKH = PBF2[:, 12288:12296]
    XVH = PBF2[:, 12296:12304]
    XRH = PBF2[:, 12304:12312]
    XNI = PF1[:, 0:4104]
    SCAN1 = PF1[:, 4104:4616]
    bank_free = {}

    def wtile(region, kg, mg):
        return region[:, (kg * 8 + mg) * P:(kg * 8 + mg) * P + P]

    m_w = []
    for reg, src in [(WKT, "wkT"), (WVT, "wvT"), (WRT, "wrT")]:
        for kg in range(G):
            m_w.append(pq.dma(reg[:, kg * 1024:(kg + 1) * 1024],
                              D[src][kg * P:(kg + 1) * P, :]))
    m_wA = m_w[-1]

    m_ln_need = [yq.dma(SMALL[:, 0:8], D["ln1g"]),
                 yq.dma(SMALL[:, 88:96], D["ln1b"])]

    MBC = PF2[:, 1040:1552]
    RBC = PF2[:, 1552:2064]

    def fm_ln_stats(src_fn, w, sq_slot0=0, shifted=None):
        sq.wait(vq.last())
        HIT = HLB[:, 0:512]
        LOT = HLB[:, 512:1024]
        m_mmprev = None
        slot_free = {}
        for g in range(G):
            slot = PF2[:, sq_slot0 + (g % 2) * 512: sq_slot0 + (g % 2) * 512 + w]
            sq.wait(slot_free.get(g % 2))
            m_sqg = act(sq, slot, src_fn(g), AF.Square, mark=True)
            if g == 0:
                tq.wait(bank_free.get(4))
                tq.wait(bank_free.get(5))
            vq.wait(m_mmprev)
            cpy(vq, HIT[:, 0:w], src_fn(g))
            m_l = tt(vq, LOT[:, 0:w], src_fn(g), HIT[:, 0:w], ALU.subtract,
                     mark=True)
            tq.wait(m_l)
            mm(PS[4][0:1, 0:w], ONECB[:], HIT[:, 0:w], start=(g == 0),
               stop=False)
            m_mmx = mm(PS[4][0:1, 0:w], ONECB[:], LOT[:, 0:w], start=False,
                       stop=(g == G - 1), mark=True)
            vq.wait(m_sqg)
            vq.wait(m_mmx)
            cpy(vq, HIT[:, 0:w], slot)
            m_l2 = tt(vq, LOT[:, 0:w], slot, HIT[:, 0:w], ALU.subtract,
                      mark=True)
            slot_free[g % 2] = m_l2
            tq.wait(m_l2)
            mm(PS[5][0:1, 0:w], ONECB[:], HIT[:, 0:w], start=(g == 0),
               stop=False)
            m_mm2 = mm(PS[5][0:1, 0:w], ONECB[:], LOT[:, 0:w], start=False,
                       stop=(g == G - 1), mark=True)
            m_mmprev = m_mm2
        sq.wait(m_mm2)
        act(sq, ROWS[0:1, 0:512][:, 0:w], PS[4][0:1, 0:w], AF.Copy,
            scale=1.0 / C)
        m_r1 = act(sq, ROWS[0:1, 512:1024][:, 0:w], PS[5][0:1, 0:w], AF.Copy,
                   scale=1.0 / C, mark=True)
        m_mu2 = act(sq, ROWS[0:1, 1024:1536][:, 0:w], ROWS[0:1, 0:512][:, 0:w],
                    AF.Square, mark=True)
        vq.wait(m_r1)
        vq.wait(m_mu2)
        m_var = tt(vq, ROWS[0:1, 1536:2048][:, 0:w],
                   ROWS[0:1, 512:1024][:, 0:w], ROWS[0:1, 1024:1536][:, 0:w],
                   ALU.subtract, mark=True)
        sq.wait(m_var)
        act(sq, ROWS[0:1, 1024:1536][:, 0:w], ROWS[0:1, 1536:2048][:, 0:w],
            AF.Ln, bias=EPSC[0:1, 0:1])
        m_rstd = act(sq, ROWS[0:1, 1024:1536][:, 0:w],
                     ROWS[0:1, 1024:1536][:, 0:w], AF.Exp, scale=-0.5,
                     mark=True)
        vq.wait(m_r1)
        vq.wait(m_rstd)
        vq.wait(m_mm2)
        cpy(vq, HLB[0:1, 0:512][:, 0:w], ROWS[0:1, 0:512][:, 0:w])
        m_hl = tt(vq, HLB[0:1, 512:1024][:, 0:w], ROWS[0:1, 0:512][:, 0:w],
                  HLB[0:1, 0:512][:, 0:w], ALU.subtract, mark=True)
        tq.wait(m_hl)
        mm(PS[4][:, 0:w], ONESBF[:], HLB[0:1, 0:512][:, 0:w], start=True,
           stop=False)
        mmu = mm(PS[4][:, 0:w], ONESBF[:], HLB[0:1, 512:1024][:, 0:w],
                 start=False, stop=True, mark=True)
        vq.wait(mmu)
        cpy(vq, HLB[0:1, 0:512][:, 0:w], ROWS[0:1, 1024:1536][:, 0:w])
        m_hl2 = tt(vq, HLB[0:1, 512:1024][:, 0:w],
                   ROWS[0:1, 1024:1536][:, 0:w], HLB[0:1, 0:512][:, 0:w],
                   ALU.subtract, mark=True)
        tq.wait(m_hl2)
        mm(PS[5][:, 0:w], ONESBF[:], HLB[0:1, 0:512][:, 0:w], start=True,
           stop=False)
        mrs = mm(PS[5][:, 0:w], ONESBF[:], HLB[0:1, 512:1024][:, 0:w],
                 start=False, stop=True, mark=True)
        sq.wait(mmu)
        sq.wait(mrs)
        sq.wait(vq.last())
        act(sq, MBC[:, 0:w], PS[4][:, 0:w], AF.Copy)
        m_bc = act(sq, RBC[:, 0:w], PS[5][:, 0:w], AF.Copy, mark=True)
        bank_free[4] = m_bc
        bank_free[5] = m_bc
        return m_bc

    sq.wait_all(m_ln_need)
    vq.wait_all(m_ln_need)
    m_bc = fm_ln_stats(lambda g: xsg(g, 256, 2), 2)
    vq.wait(m_bc)
    for g in range(G):
        dst = vcols(160 + 2 * g, 2)
        tt(vq, dst, xsg(g, 256, 2), MBC[:, 0:2], ALU.subtract)
        tt(vq, dst, dst, RBC[:, 0:2], ALU.mult)
        ts(vq, dst, dst, SMALL[:, 0 + g:1 + g], SMALL[:, 88 + g:89 + g],
           ALU.mult, ALU.add)
        ts(vq, dst, dst, SMALL[:, 64:65], None, ALU.mult, ALU.add)
    m_mixh = None
    for g in range(G):
        xn1 = vcols(161 + 2 * g, 1)
        xn0 = vcols(160 + 2 * g, 1)
        for dst, ai, oi in [(XKH, 24, 24), (XVH, 32, 32), (XRH, 40, 40)]:
            ts(vq, vcols(176, 1), xn0, vcols(oi, 8)[:, g:g + 1], None,
               ALU.mult, ALU.add)
            m_mixh = stt(vq, dst[:, g:g + 1], xn1,
                         SMALL[:, ai + g:ai + g + 1], vcols(176, 1),
                         ALU.mult, ALU.add, mark=True)
    tq.wait(m_mixh)
    tq.wait_all(m_w[7:8] + [m_w[15], m_w[23]])
    for reg, xh, fn, dstc in [(WKT, XKH, AF.Exp, 112), (WVT, XVH, AF.Copy, 120),
                              (WRT, XRH, AF.Sigmoid, 128)]:
        for mg in range(G):
            bank = PS[mg % 2]
            tq.wait(bank_free.get(mg % 2))
            for kg in range(G):
                m_mmh = mm(bank[:, 0:1], wtile(reg, kg, mg), xh[:, kg:kg + 1],
                           start=(kg == 0), stop=(kg == G - 1),
                           mark=(kg == G - 1))
            sq.wait(m_mmh)
            m_ev = act(sq, vcols(dstc + mg, 1), bank[:, 0:1], fn, mark=True)
            bank_free[mg % 2] = m_ev

    if stage == "A1a":
        dbg_carry = nc.dram_tensor("dbg_carry", [P, 184], F32,
                                   kind="ExternalOutput").ap()
        yq.dma(dbg_carry, VEC[:, 0:184])
        yq.dma(out, D["x_own"])
        yq.wait(yq.last_dma())
        return emit()

    vq.wait(sq.last())
    ts(vq, vcols(112, 8), vcols(112, 8), SMALL[:, 64:65], None, ALU.mult,
       ALU.add)

    memset(vq, vcols(64, 16), 0.0)
    NCH1 = 4
    m_estg = {}
    m_vstg = {}
    m_xni_users = None
    for ch in range(NCH1):
        cb = OWN0 + 512 * ch
        m_bc = fm_ln_stats(lambda g: xsg(g, cb, 512), 512)
        vq.wait(m_bc)
        for g in range(G):
            dst = XNI[:, g * 513 + 1: g * 513 + 513]
            vq.wait(m_xni_users)
            tt(vq, dst, xsg(g, cb, 512), MBC, ALU.subtract)
            tt(vq, dst, dst, RBC, ALU.mult)
            m_xni = ts(vq, dst, dst, SMALL[:, 0 + g:1 + g],
                       SMALL[:, 88 + g:89 + g], ALU.mult, ALU.add, mark=True)
            if ch == 0:
                m_b = cpy(vq, XNI[:, g * 513:g * 513 + 1],
                          vcols(161 + 2 * g, 1), mark=True)
            else:
                m_b = cpy(vq, XNI[:, g * 513:g * 513 + 1],
                          vcols(152 + g, 1), mark=True)
        m_mix = None
        for g in range(G):
            sh = XNI[:, g * 513: g * 513 + 512]
            cur = XNI[:, g * 513 + 1: g * 513 + 513]
            for dst, ai, oi in [(XKB, 24, 24), (XVB, 32, 32), (XRB, 40, 40)]:
                tmp = PF2[:, 0:512]
                ts(vq, tmp, sh, vcols(oi, 8)[:, g:g + 1], None, ALU.mult,
                   ALU.add)
                m_mix = stt(vq, dst[:, g * 512:(g + 1) * 512], cur,
                            SMALL[:, ai + g:ai + g + 1], tmp, ALU.mult,
                            ALU.add, mark=True)
            cpy(vq, vcols(152 + g, 1), XNI[:, g * 513 + 512:g * 513 + 513])
        m_xni_users = None
        tq.wait(m_mix)
        tq.wait_all([m_wA])
        m_scan_prev = None
        for mg in range(G):
            par = mg % 2
            eslot = ESTG[:, 0:512]
            vslot = VSTG[:, 0:512]
            tq.wait(bank_free.get(par))
            sq.wait(m_estg.get("e"))
            sq.wait(m_estg.get("v"))
            for kg in range(G):
                mk = mm(PS[par][:, 0:512], wtile(WKT, kg, mg),
                        XKB[:, kg * 512:(kg + 1) * 512],
                        start=(kg == 0), stop=(kg == G - 1),
                        mark=(kg == G - 1))
            sq.wait(mk)
            m_e = act(sq, eslot, PS[par][:, 0:512], AF.Exp, mark=True)
            bank_free[par] = m_e
            tq.wait(m_e)
            par2 = 2 + (mg % 2)
            tq.wait(bank_free.get(par2))
            tq.wait(m_vstg.get(par2))
            for kg in range(G):
                mv = mm(PS[par2][:, 0:512], wtile(WVT, kg, mg),
                        XVB[:, kg * 512:(kg + 1) * 512],
                        start=(kg == 0), stop=(kg == G - 1),
                        mark=(kg == G - 1))
            sq.wait(mv)
            m_v = act(sq, vslot, PS[par2][:, 0:512], AF.Copy, mark=True)
            bank_free[par2] = m_v
            yq.wait(m_e)
            me_d = yq.dma(
                espill[mg * P:(mg + 1) * P, 512 * ch:512 * ch + 512], eslot)
            yq.wait(m_v)
            mv_d = yq.dma(
                vspill[mg * P:(mg + 1) * P, 512 * ch:512 * ch + 512], vslot)
            vq.wait(m_e)
            vq.wait(m_v)
            vq.wait(m_scan_prev)
            tt(vq, SCAN1, eslot, vslot, ALU.mult)
            scan(vq, SCAN1, vcols(8, 8)[:, mg:mg + 1].broadcast_to([P, 512]),
                 SCAN1, vcols(64 + mg, 1), )
            cpy(vq, vcols(64 + mg, 1), SCAN1[:, 511:512])
            if ch == NCH1 - 1:
                cpy(vq, SMALL[:, 112 + mg:113 + mg], SCAN1[:, 510:511])
            m_sb = scan(vq, SCAN1,
                        vcols(8, 8)[:, mg:mg + 1].broadcast_to([P, 512]),
                        eslot, vcols(72 + mg, 1), mark=True)
            cpy(vq, vcols(72 + mg, 1), SCAN1[:, 511:512])
            if ch == NCH1 - 1:
                m_scan_prev = cpy(vq, SMALL[:, 120 + mg:121 + mg],
                                  SCAN1[:, 510:511], mark=True)
            else:
                m_scan_prev = m_sb
            m_estg["e"] = [me_d, m_scan_prev]
            m_estg["v"] = [mv_d, m_scan_prev]
        for mg in range(G):
            par = 4 + (mg % 2)
            slot = SSTG[:, 0:512] if mg % 2 == 0 else ESTG[:, 0:512]
            skey = "sr" + str(mg % 2)
            tq.wait(bank_free.get(par))
            for kg in range(G):
                mr = mm(PS[par][:, 0:512], wtile(WRT, kg, mg),
                        XRB[:, kg * 512:(kg + 1) * 512],
                        start=(kg == 0), stop=(kg == G - 1),
                        mark=(kg == G - 1))
            sq.wait(mr)
            sq.wait(m_estg.get(skey))
            if mg % 2 == 1:
                sq.wait(m_estg.get("e"))
            m_sr = act(sq, slot, PS[par][:, 0:512], AF.Sigmoid, mark=True)
            bank_free[par] = m_sr
            yq.wait(m_sr)
            m_srd = yq.dma(srspill[mg * P:(mg + 1) * P,
                                   512 * ch:512 * ch + 512], slot)
            m_estg[skey] = m_srd
            if mg % 2 == 1:
                m_estg["e"] = [m_estg.get("e"), m_srd]
        m_xni_users = tq.last()

    if stage == "A1b":
        dbg_e = nc.dram_tensor("dbg_e", [C, T], F32,
                               kind="ExternalOutput").ap()
        dbg_carry = nc.dram_tensor("dbg_carry", [P, 184], F32,
                                   kind="ExternalOutput").ap()
        dbg_xni = nc.dram_tensor("dbg_xni", [P, 4104], F32,
                                 kind="ExternalOutput").ap()
        dbg_pf2 = nc.dram_tensor("dbg_pf2", [P, 2112], F32,
                                 kind="ExternalOutput").ap()
        cx.barrier()
        yq.dma(dbg_e, espill)
        yq.dma(dbg_carry, VEC[:, 0:184])
        yq.dma(dbg_xni, PF1[:, 0:4104])
        yq.dma(dbg_pf2, PF2[:])
        yq.dma(out, D["x_own"])
        yq.wait(yq.last_dma())
        return emit()

    cpy(vq, SMALL[:, 96:104], vcols(64, 8))
    m_gc = cpy(vq, SMALL[:, 104:112], vcols(72, 8), mark=True)
    yq.wait(m_gc)
    m_gin = yq.dma(gwkv_in.ap(), SMALL[:, 96:128])
    pq.wait(m_gin)
    m_cc = cx.collective("AllGather", ALU.bypass, [list(range(NCORES))],
                         [gwkv_in.ap().opt()], [gwkv_out.ap().opt()])
    yq.wait(m_cc)
    m_gw = []
    for r in range(NCORES):
        m_gw.append(yq.dma(GW[:, r * 32:(r + 1) * 32],
                           gwkv_out.ap()[r * P:(r + 1) * P, :]))
    m_steps = [yq.dma(SMALL[:, 144:152], D["stepsA"]),
               yq.dma(SMALL[:, 152:160], D["stepsH1"]),
               yq.dma(SMALL[:, 160:168], D["stepsH2"])]
    cx.barrier()

    GW3 = GW[:].rearrange("p (r c) -> p r c", r=NCORES)

    def carry_combine(steps_ap, vec_off, dst_col):
        for g in range(G):
            W8 = SMALL[:, 128:136]
            m_w8 = ts(vq, W8, steps_ap, vcols(0, 8)[:, g:g + 1], None,
                      ALU.mult, ALU.add, mark=True)
            sq.wait(m_w8)
            m_ex = act(sq, W8, W8, AF.Exp, mark=True)
            vq.wait(m_ex)
            tt(vq, W8, W8, GW3[:, :, vec_off * 8 + g], ALU.mult)
            redu(vq, dst_col(g), W8, ALU.add)

    carry_combine(SMALL[:, 144:152], 0, lambda g: vcols(80 + g, 1))
    carry_combine(SMALL[:, 144:152], 1, lambda g: vcols(88 + g, 1))
    carry_combine(SMALL[:, 152:160], 0, lambda g: vcols(96 + g, 1))
    carry_combine(SMALL[:, 152:160], 1, lambda g: vcols(104 + g, 1))
    for g in range(G):
        W8 = SMALL[:, 128:136]
        m_w8 = ts(vq, W8, SMALL[:, 160:168], vcols(0, 8)[:, g:g + 1], None,
                  ALU.mult, ALU.add, mark=True)
        sq.wait(m_w8)
        m_ex = act(sq, W8, W8, AF.Exp, mark=True)
        vq.wait(m_ex)
        tt(vq, SMALL[:, 136:144], W8, GW3[:, :, 2 * 8 + g], ALU.mult)
        redu(vq, SMALL[:, 176:177], SMALL[:, 136:144], ALU.add)
        tt(vq, vcols(96 + g, 1), vcols(96 + g, 1), SMALL[:, 176:177], ALU.add)
        tt(vq, SMALL[:, 136:144], W8, GW3[:, :, 3 * 8 + g], ALU.mult)
        redu(vq, SMALL[:, 176:177], SMALL[:, 136:144], ALU.add)
        m_carry = tt(vq, vcols(104 + g, 1), vcols(104 + g, 1),
                     SMALL[:, 176:177], ALU.add, mark=True)
    cx.barrier()

    if stage == "A1":
        dbg_e = nc.dram_tensor("dbg_e", [C, T], F32,
                               kind="ExternalOutput").ap()
        dbg_carry = nc.dram_tensor("dbg_carry", [P, 184], F32,
                                   kind="ExternalOutput").ap()
        yq.dma(dbg_e, espill)
        yq.dma(dbg_carry, VEC[:, 0:184])
        yq.dma(out, D["x_own"])
        yq.wait(yq.last_dma())
        return emit()

    WOT = PBF[:, 0:8192]
    XK2B = PBF[:, 8192:10240]
    XR2B = PBF[:, 10240:12288]
    RWB = PBF[:, 12288:14352]
    WRF1 = PBF[:, 14352:15376]
    WVF1 = PBF[:, 15376:19472]
    WKF1 = PBF[:, 19472:20496]
    KFB = PBF2[:, 0:8192]
    XACH = PF1[:, 0:2080]
    XN2 = PF1[:, 2080:4160]
    SR2CH = PF1[:, 4160:6208]
    SCA = PF1[:, 6208:6468]
    SCB = PF1[:, 6468:6728]
    NUM = PF1[:, 6728:6988]
    DEN = PF1[:, 6988:7248]
    YT = PF1[:, 7248:7508]
    EVS = PF2[:, 0:260]
    VVS = PF2[:, 260:520]
    SRS = PF2[:, 520:780]
    TMPQ = PF2[:, 780:1036]

    m_wo = []
    for kg in range(G):
        m_wo.append(pq.dma(WOT[:, kg * 1024:(kg + 1) * 1024],
                           D["woT"][kg * P:(kg + 1) * P, :]))
    if stage == "A":
        dbg_wot_early = nc.dram_tensor("dbg_wot_early", [P, 8192], BF16,
                                       kind="ExternalOutput").ap()
        yq.wait(m_wo[-1])
        yq.dma(dbg_wot_early, WOT)
    m_ln2 = [yq.dma(SMALL[:, 0:8], D["ln2g"]),
             yq.dma(SMALL[:, 88:96], D["ln2b"])]

    NCH2 = 8
    m_rwb_users = None
    m_xach_users = None
    m_sr2_users = None
    wotprobe = {"n": 0, "buf": None}
    if stage == "A":
        wotprobe["buf"] = nc.dram_tensor("dbg_wotp", [12, P, 8192], BF16,
                                         kind="ExternalOutput").ap()

    def wot_probe():
        if wotprobe["buf"] is None or wotprobe["n"] >= 12:
            return
        i = wotprobe["n"]
        wotprobe["n"] += 1
        yq.wait(tq.last())
        yq.wait(vq.last())
        yq.wait(sq.last())
        yq.wait(pq.last_dma())
        yq.dma(wotprobe["buf"][i], WOT)

    for ch in range(NCH2):
        W = 257 if ch == 0 else 256
        cb = BA0 if ch == 0 else BA0 + 1 + 256 * ch
        sp0 = 256 * ch
        for g in range(G):
            o = 1 if ch == 0 else 0
            m_ld = []
            if ch == 0:
                cpy(vq, EVS[:, 0:1], vcols(112 + g, 1))
                cpy(vq, VVS[:, 0:1], vcols(120 + g, 1))
                m_srsan = cpy(vq, SRS[:, 0:1], vcols(128 + g, 1), mark=True)
                yq.wait(m_srsan)
                m_ld.append(yq.dma(EVS[:, 1:257],
                                   espill[g * P:(g + 1) * P, 0:256]))
                m_ld.append(yq.dma(VVS[:, 1:257],
                                   vspill[g * P:(g + 1) * P, 0:256]))
                m_ld.append(yq.dma(SRS[:, 1:257],
                                   srspill[g * P:(g + 1) * P, 0:256]))
            else:
                m_ld.append(yq.dma(EVS[:, 0:256],
                                   espill[g * P:(g + 1) * P, sp0:sp0 + 256]))
                m_ld.append(yq.dma(VVS[:, 0:256],
                                   vspill[g * P:(g + 1) * P, sp0:sp0 + 256]))
                m_ld.append(yq.dma(SRS[:, 0:256],
                                   srspill[g * P:(g + 1) * P, sp0:sp0 + 256]))
            vq.wait_all(m_ld)
            tt(vq, NUM[:, 0:W], EVS[:, 0:W], VVS[:, 0:W], ALU.mult)
            initA = vcols(96 + g, 1) if ch == 0 else vcols(80 + g, 1)
            initB = vcols(104 + g, 1) if ch == 0 else vcols(88 + g, 1)
            cpy(vq, SCA[:, 0:1], initA)
            scan(vq, SCA[:, 1:W + 1],
                 vcols(8, 8)[:, g:g + 1].broadcast_to([P, W]), NUM[:, 0:W],
                 initA)
            cpy(vq, vcols(80 + g, 1), SCA[:, W:W + 1])
            cpy(vq, SCB[:, 0:1], initB)
            scan(vq, SCB[:, 1:W + 1],
                 vcols(8, 8)[:, g:g + 1].broadcast_to([P, W]), EVS[:, 0:W],
                 initB)
            cpy(vq, vcols(88 + g, 1), SCB[:, W:W + 1])
            ts(vq, EVS[:, 0:W], EVS[:, 0:W], vcols(16 + g, 1), None, ALU.mult,
               ALU.add)
            tt(vq, NUM[:, 0:W], EVS[:, 0:W], VVS[:, 0:W], ALU.mult)
            tt(vq, NUM[:, 0:W], NUM[:, 0:W], SCA[:, 0:W], ALU.add)
            m_den = tt(vq, DEN[:, 0:W], EVS[:, 0:W], SCB[:, 0:W], ALU.add,
                       mark=True)
            if ch == 0:
                m_den = ts(vq, DEN[:, 0:1], DEN[:, 0:1], 1.0, 1e-30,
                           ALU.mult, ALU.add, mark=True)
            sq.wait(m_den)
            act(sq, DEN[:, 0:W], DEN[:, 0:W], AF.Ln)
            m_rd = act(sq, DEN[:, 0:W], DEN[:, 0:W], AF.Exp, scale=-1.0,
                       mark=True)
            vq.wait(m_rd)
            m_yt = tt(vq, YT[:, 0:W], NUM[:, 0:W], DEN[:, 0:W], ALU.mult,
                      mark=True)
            yq.wait(m_yt)
            oy = 0 if ch == 0 else 256 * ch + 1
            yq.dma(yspill[g * P:(g + 1) * P, oy:oy + W], YT[:, 0:W])
            vq.wait(m_rwb_users)
            m_rw = tt(vq, RWB[:, g * 258:g * 258 + W], YT[:, 0:W],
                      SRS[:, 0:W], ALU.mult, mark=True)
        if stage == "A" and ch == 1:
            dbg_rwb1 = nc.dram_tensor("dbg_rwb1", [P, 2064], BF16,
                                      kind="ExternalOutput").ap()
            yq.wait(m_rw)
            yq.dma(dbg_rwb1, RWB)
        xa_spilled = {}
        tq.wait(m_rw)
        tq.wait_all(m_wo[-1:])
        for mg in range(G):
            bank = PS[mg % 4]
            tq.wait(bank_free.get(mg % 4))
            for kg in range(G):
                mw = mm(bank[:, 0:W], wtile(WOT, kg, mg),
                        RWB[:, kg * 258:kg * 258 + W],
                        start=(kg == 0), stop=(kg == G - 1),
                        mark=(kg == G - 1))
            vq.wait(mw)
            if stage == "A" and ch == 1 and mg == 0:
                dbg_ps1 = nc.dram_tensor("dbg_ps1", [P, 256], F32,
                                         kind="ExternalOutput").ap()
                dbg_xs1 = nc.dram_tensor("dbg_xs1", [P, 256], F32,
                                         kind="ExternalOutput").ap()
                sq.wait(mw)
                m_pse = act(sq, PF2[:, 780:1036], bank[:, 0:W], AF.Copy,
                            mark=True)
                yq.wait(m_pse)
                yq.dma(dbg_ps1, PF2[:, 780:1036])
                yq.dma(dbg_xs1, xsg(0, cb, W))
                vq.wait(m_pse)
            vq.wait(m_xach_users)
            m_xa = tt(vq, XACH[:, mg * 260:mg * 260 + W], bank[:, 0:W],
                      xsg(mg, cb, W), ALU.add, mark=True)
            bank_free[mg % 4] = m_xa
            yq.wait(m_xa)
            oy = 0 if ch == 0 else 256 * ch + 1
            m_xad = yq.dma(xaspill[mg * P:(mg + 1) * P, oy:oy + W],
                           XACH[:, mg * 260:mg * 260 + W])
            xa_spilled[mg] = m_xad
        m_rwb_users = tq.last()
        sq.wait_all(m_ln2)
        m_bc = fm_ln_stats(lambda g: XACH[:, g * 260:g * 260 + W], W)
        vq.wait(m_bc)
        for g in range(G):
            src = XACH[:, g * 260:g * 260 + W]
            if ch == 0:
                dst = XN2[:, g * 260:g * 260 + W]
            else:
                cpy(vq, XN2[:, g * 260:g * 260 + 1], vcols(144 + g, 1))
                dst = XN2[:, g * 260 + 1:g * 260 + 1 + W]
            tt(vq, dst, src, MBC[:, 0:W], ALU.subtract)
            tt(vq, dst, dst, RBC[:, 0:W], ALU.mult)
            ts(vq, dst, dst, SMALL[:, 0 + g:1 + g], SMALL[:, 88 + g:89 + g],
               ALU.mult, ALU.add)
            if ch == 0:
                ts(vq, XN2[:, g * 260:g * 260 + 1],
                   XN2[:, g * 260:g * 260 + 1], SMALL[:, 64:65], None,
                   ALU.mult, ALU.add)
            wend = g * 260 + (W if ch == 0 else 1 + W)
            m_bnd = cpy(vq, vcols(144 + g, 1), XN2[:, wend - 1:wend],
                        mark=True)
        for g in range(G):
            base = g * 260
            sh = XN2[:, base:base + 256]
            cur = XN2[:, base + 1:base + 257]
            for dstr, ai, oi in [(XK2B, 48, 48), (XR2B, 56, 56)]:
                ts(vq, TMPQ, sh, vcols(oi, 8)[:, g:g + 1], None, ALU.mult,
                   ALU.add)
                m_mix2 = stt(vq, dstr[:, g * 256:(g + 1) * 256], cur,
                             SMALL[:, ai + g:ai + g + 1], TMPQ, ALU.mult,
                             ALU.add, mark=True)
        tq.wait(m_mix2)
        mr2 = None
        for mg in range(G):
            pq.wait(mr2)
            m_wrf = pq.dma(WRF1[:].rearrange("p (kg m) -> p kg m", m=P),
                           D["wrfT"][:, mg * P:(mg + 1) * P].rearrange(
                               "(kg p) m -> p kg m", p=P))
            tq.wait(m_wrf)
            bank = PS[4 + mg % 2]
            tq.wait(bank_free.get(4 + mg % 2))
            for kg in range(G):
                mr2 = mm(bank[:, 0:256], WRF1[:, kg * P:(kg + 1) * P],
                         XR2B[:, kg * 256:(kg + 1) * 256],
                         start=(kg == 0), stop=(kg == G - 1),
                         mark=(kg == G - 1))
            sq.wait(mr2)
            sq.wait(m_sr2_users)
            m_sr2 = act(sq, SR2CH[:, mg * 256:(mg + 1) * 256], bank[:, 0:256],
                        AF.Sigmoid, mark=True)
            bank_free[4 + mg % 2] = m_sr2
        mkf = None
        for fc in range(32):
            pq.wait(mkf)
            m_wkf = pq.dma(WKF1[:].rearrange("p (kg m) -> p kg m", m=P),
                           D["wkfT"][:, fc * P:(fc + 1) * P]
                           .rearrange("(kg p) m -> p kg m", p=P))
            tq.wait(m_wkf)
            bank = PS[4 + fc % 2]
            tq.wait(bank_free.get(4 + fc % 2))
            for kg in range(G):
                mkf = mm(bank[:, 0:256], WKF1[:, kg * P:(kg + 1) * P],
                         XK2B[:, kg * 256:(kg + 1) * 256],
                         start=(kg == 0), stop=(kg == G - 1),
                         mark=(kg == G - 1))
            sq.wait(mkf)
            act(sq, TMPQ, bank[:, 0:256], AF.Relu)
            m_kf = act(sq, KFB[:, fc * 256:(fc + 1) * 256], TMPQ, AF.Square,
                       mark=True)
            bank_free[4 + fc % 2] = m_kf
        tq.wait(m_kf)
        mkv = None
        for mg in range(G):
            pq.wait(mkv)
            m_wvf = pq.dma(WVF1[:].rearrange("p (fc m) -> p fc m", m=P),
                           D["wvfT"][:, mg * P:(mg + 1) * P]
                           .rearrange("(fc p) m -> p fc m", p=P))
            tq.wait(m_wvf)
            bank = PS[mg % 4]
            tq.wait(bank_free.get(mg % 4))
            for fc in range(32):
                mkv = mm(bank[:, 0:256], WVF1[:, fc * P:(fc + 1) * P],
                         KFB[:, fc * 256:(fc + 1) * 256],
                         start=(fc == 0), stop=(fc == 31),
                         mark=(fc == 31))
            vq.wait(mkv)
            vq.wait(xa_spilled.get(mg))
            tt(vq, TMPQ, SR2CH[:, mg * 256:(mg + 1) * 256], bank[:, 0:256],
               ALU.mult)
            xa_own = XACH[:, mg * 260 + (1 if ch == 0 else 0):
                          mg * 260 + (1 if ch == 0 else 0) + 256]
            m_s = tt(vq, xa_own, TMPQ, xa_own, ALU.add, mark=True)
            bank_free[mg % 4] = m_s
            yq.wait(m_s)
            m_sd = yq.dma(sspill[mg * P:(mg + 1) * P, sp0:sp0 + 256], xa_own)
        m_xach_users = yq.last_dma()
        m_sr2_users = vq.last()
        cx.barrier()
        wot_probe()

    if stage == "full":
        KR0, VR0, QB0, WSA0, WSB0 = 0, 8192, 16384, 22528, 23552
        HH0, XNB0, W1A0, W1G0 = 0, 11264, 15360, 16384
        AOB0, PTS0, MR0, MW0, PEX0, XBC0 = 0, 4096, 4608, 5632, 6656, 7680
        SQB0, SELB0, W2S0 = 11776, 12300, 0

        def p2(c0, w):
            return PBF2[:, c0:c0 + w]

        def pb(c0, w):
            return PBF[:, c0:c0 + w]

        def krp(p, c0, w):
            return PBF[:, KR0 + p * 1024 + c0:KR0 + p * 1024 + c0 + w]

        def qbp(p, c0, w):
            return PBF[:, QB0 + p * 768 + c0:QB0 + p * 768 + c0 + w]

        def vrb(b_):
            c0 = VR0 + (b_ % 8) * 1024
            return PBF[:, c0:c0 + 1024]

        TB_A = PF1[:, 0:512]
        TB_B = PF1[:, 512:1024]
        TB_AI = PF1[:, 1024:1536]
        TB_BI = PF1[:, 1536:2048]
        UQT = PF1[:, 2048:2560]
        WQZ = PF1[:, 2560:3072]
        RCF = PF1[:, 3072:3584]
        SCB = PF1[:, 3584:4608]
        GBA = PF1[:, 4608:4866]
        GST = PF1[:, 4866:5124]
        B1A = PF1[:, 5124:5146]
        B1G = PF1[:, 5146:5168]
        B2C = PF1[:, 5168:5176]
        LNGC = PF1[:, 5176:5184]
        LNBC = PF1[:, 5184:5192]
        QSCC = PF1[:, 5192:5193]
        KSCC = PF1[:, 5193:5194]
        GTT = PF1[:, 5248:5760]
        FTMP = PF1[:, 5760:6272]
        QFT = PF2[:, 0:512]
        KFT = PF2[:, 512:1024]
        RSB2 = PF2[:, 1024:1536]
        PXH = PBF2[0:2, PEX0:PEX0 + 512]
        PXL = PBF2[0:2, PEX0 + 512:PEX0 + 1024]

        m_bmsk = [pq.dma(p2(MR0, 1024), D["maskreg"]),
                  pq.dma(p2(MW0, 1024), D["maskw0"]),
                  pq.dma(p2(SELB0, 2), D["sel64"])]
        memset(vq, SMALL[:, 176:177], 8.0)
        tq.wait_all(m_bmsk)
        vq.wait_all(m_bmsk)

        ws_state = {WSA0: None, WSB0: None, W1A0: None, W1G0: None,
                    "w2": None}
        ws_flip = [0]

        def wstream(src2d, r0, r1):
            slot = (WSA0, WSB0)[ws_flip[0] % 2]
            ws_flip[0] += 1
            pq.wait(ws_state[slot])
            m_w = pq.dma(pb(slot, 1024).rearrange("p (kg m) -> p kg m", m=P),
                         src2d.rearrange("(kg p) m -> p kg m", p=P))
            return slot, m_w

        l2_state = {"mm1": None, "mm5": None, "rs_user": None}

        def l2rs(w, m_sq):
            tq.wait(m_sq)
            m1 = mm(PSROW[:, 0:w], p2(SELB0, 2), p2(SQB0, w), start=True,
                    stop=True, mark=True)
            l2_state["mm1"] = m1
            vq.wait(m1)
            m2 = ts(vq, ROWS[0:2, 0:w], PSROW[:, 0:w], 1e-24, None, ALU.max,
                    ALU.add, mark=True)
            sq.wait(m2)
            act(sq, ROWS[0:2, 0:w], ROWS[0:2, 0:w], AF.Ln)
            m3 = act(sq, ROWS[0:2, 0:w], ROWS[0:2, 0:w], AF.Exp, scale=-0.5,
                     mark=True)
            vq.wait(m3)
            vq.wait(l2_state["mm5"])
            cpy(vq, PXH[:, 0:w], ROWS[0:2, 0:w])
            m4 = tt(vq, PXL[:, 0:w], ROWS[0:2, 0:w], PXH[:, 0:w],
                    ALU.subtract, mark=True)
            tq.wait(m4)
            tq.wait(bank_free.get(5))
            mm(PS[5][:, 0:w], SELT2B[:], PXH[:, 0:w], start=True, stop=False)
            m5 = mm(PS[5][:, 0:w], SELT2B[:], PXL[:, 0:w], start=False,
                    stop=True, mark=True)
            l2_state["mm5"] = m5
            sq.wait(m5)
            sq.wait(l2_state["rs_user"])
            m6 = act(sq, RSB2[:, 0:w], PS[5][:, 0:w], AF.Copy, mark=True)
            bank_free[5] = m6
            return m6

        def rot_quarters(dst, u, wps, wd):
            tt(vq, dst[0:32, 0:wd], u[0:32, 0:wd], wps[32:64, 0:wd],
               ALU.subtract)
            tt(vq, dst[32:64, 0:wd], u[32:64, 0:wd], wps[0:32, 0:wd],
               ALU.add)
            tt(vq, dst[64:96, 0:wd], u[64:96, 0:wd], wps[96:128, 0:wd],
               ALU.subtract)
            return tt(vq, dst[96:128, 0:wd], u[96:128, 0:wd],
                      wps[64:96, 0:wd], ALU.add, mark=True)

        NQC = 5
        m_sqb_user = None
        m_qft_user = None
        m_kft_user = None
        for l in range(nl_build):
            lr = l * C
            m_lv = [yq.dma(B1A, D["b1a"][l * P:(l + 1) * P, :]),
                    yq.dma(B1G, D["b1g"][l * P:(l + 1) * P, :]),
                    yq.dma(B2C, D["b2v"][l * P:(l + 1) * P, :]),
                    yq.dma(LNGC, D["lngv"][l * P:(l + 1) * P, :]),
                    yq.dma(LNBC, D["lnbv"][l * P:(l + 1) * P, :]),
                    yq.dma(QSCC, D["qsv"][l * P:(l + 1) * P, :]),
                    yq.dma(KSCC, D["ksv"][l * P:(l + 1) * P, :])]
            vq.wait_all(m_lv)
            sq.wait_all(m_lv)
            if l > 0:
                m_bnd = []
                for g in range(G):
                    m_bnd.append(yq.dma(gbb_in.ap()[g * P:(g + 1) * P, :],
                                        xsg(g, 2048, HALO)))
                pq.wait_all(m_bnd)
                m_cc = cx.collective("AllGather", ALU.bypass,
                                     [list(range(NCORES))],
                                     [gbb_in.ap().opt()],
                                     [gbb_out.ap().opt()])
                yq.wait(m_cc)
                for g in range(G):
                    m_z = memset(vq, GBA[:], 0.0, mark=True)
                    for r in range(NCORES):
                        m_gl = yq.dma(GST[:],
                                      gbb_out.ap()[r * C + g * P:
                                                   r * C + (g + 1) * P, :])
                        vq.wait(m_gl)
                        m_acc = stt(vq, GBA[:], GST[:],
                                    SMALL[:, 72 + r:73 + r], GBA[:],
                                    ALU.mult, ALU.add, mark=True)
                        yq.wait(m_acc)
                    cpy(vq, xsg(g, 0, HALO), GBA[:], mark=True)
                cx.barrier()

            m_ao_last = None
            m_wo_user = None
            for mc in range(NQC):
                c0 = 2 + 512 * mc
                W = min(514 + 512 * mc, EX) - c0
                m_xbc = None
                for kg in range(G):
                    off = 1 if kg >= 4 else 0
                    m_xbc = cpy(vq, p2(XBC0 + kg * 512, W),
                                xsg(kg, c0 - off, W), mark=True)
                m_tb = [yq.dma(TB_A[:, 0:W], D["atab"][:, c0:c0 + W]),
                        yq.dma(TB_B[:, 0:W], D["btab"][:, c0:c0 + W]),
                        yq.dma(TB_AI[:, 0:W], D["aitab"][:, c0:c0 + W]),
                        yq.dma(TB_BI[:, 0:W], D["bitab"][:, c0:c0 + W])]
                vq.wait_all(m_tb)
                for p in range(G):
                    for side in range(2):
                        dst = (QFT, KFT)[side]
                        m0 = side * 1024 + p * P
                        slot, m_w = wstream(
                            D["wqkv"][lr:lr + C, m0:m0 + P], 0, 0)
                        tq.wait(m_w)
                        tq.wait(m_xbc)
                        tq.wait(bank_free.get(0))
                        for kg in range(G):
                            m_qk = mm(PS[0][:, 0:W],
                                      pb(slot + kg * P, P),
                                      p2(XBC0 + kg * 512, W),
                                      start=(kg == 0), stop=(kg == G - 1),
                                      mark=(kg == G - 1))
                        ws_state[slot] = m_qk
                        sq.wait(m_qk)
                        sq.wait(m_qft_user if side == 0 else m_kft_user)
                        m_ev = act(sq, dst[:, 0:W], PS[0][:, 0:W], AF.Copy,
                                   mark=True)
                        bank_free[0] = m_ev
                        sq.wait(m_sqb_user)
                        m_sq = act(sq, p2(SQB0, W), dst[:, 0:W], AF.Square,
                                   mark=True)
                        m_rs = l2rs(W, m_sq)
                        m_sqb_user = l2_state["mm1"]
                        vq.wait(m_ev)
                        vq.wait(m_sq)
                        ts(vq, dst[:, 0:W], dst[:, 0:W],
                           (QSCC, KSCC)[side], None, ALU.mult, ALU.add)
                        tt(vq, UQT[:, 0:W], dst[:, 0:W],
                           (TB_A, TB_AI)[side][:, 0:W], ALU.mult)
                        vq.wait(bank_free.get(1))
                        m_last_rd = tt(vq, PS[1][:, 0:W], dst[:, 0:W],
                                       (TB_B, TB_BI)[side][:, 0:W],
                                       ALU.mult, mark=True)
                        if side == 0:
                            m_qft_user = m_last_rd
                        else:
                            m_kft_user = m_last_rd
                        bank_free[1] = rot_quarters(RCF, UQT, PS[1], W)
                        vq.wait(m_rs)
                        if side == 1:
                            kcol = (2 * mc % 4) * 256
                            m_rn = tt(vq, krp(p, kcol, W), RCF[:, 0:W],
                                      RSB2[:, 0:W], ALU.mult, mark=True)
                        else:
                            m_rn = tt(vq, p2(SQB0, W), RCF[:, 0:W],
                                      RSB2[:, 0:W], ALU.mult, mark=True)
                            for si, wslot in enumerate((2 * mc - 1, 2 * mc)):
                                if wslot < 0 or wslot > 7 or si * 256 >= W:
                                    continue
                                m_sqb_user = cpy(
                                    vq, qbp(p, (wslot % 3) * 256, 256),
                                    p2(SQB0 + si * 256, 256), mark=True)
                        l2_state["rs_user"] = m_rn
                for mg in range(G):
                    slot, m_w = wstream(
                        D["wqkv"][lr:lr + C, 2048 + mg * P:2048 + (mg + 1) * P],
                        0, 0)
                    tq.wait(m_w)
                    tq.wait(bank_free.get(1))
                    for kg in range(G):
                        m_vv = mm(PS[1][:, 0:W], pb(slot + kg * P, P),
                                  p2(XBC0 + kg * 512, W),
                                  start=(kg == 0), stop=(kg == G - 1),
                                  mark=(kg == G - 1))
                    ws_state[slot] = m_vv
                    sq.wait(m_vv)
                    sq.wait(m_sqb_user)
                    m_vf = act(sq, p2(SQB0, W), PS[1][:, 0:W], AF.Copy,
                               mark=True)
                    bank_free[1] = m_vf
                    tq.wait(m_vf)
                    for tb in range((W + P - 1) // P):
                        tq.wait(bank_free.get("tb"))
                        m_tr = tq.run(
                            lambda e, tb=tb: e.transpose(
                                PSTB[:, 0:P],
                                PBF2[:, SQB0 + tb * P:SQB0 + (tb + 1) * P],
                                IDENTB[:]), mark=True)
                        m_sqb_user = m_tr
                        sq.wait(m_tr)
                        m_vt = act(sq, vrb(4 * mc + tb)[:, mg * P:
                                                        (mg + 1) * P],
                                   PSTB[:, 0:P], AF.Copy, mark=True)
                        bank_free["tb"] = m_vt
                cx.barrier()

                def fire(w):
                    nonlocal m_ao_last
                    mbase = MW0 if w == 0 else MR0
                    aoc = (w % 2) * 256
                    for p in range(G):
                        for qh in range(2):
                            qc = (w % 3) * 256 + qh * P
                            for head in range(2):
                                tq.wait(bank_free.get(head))
                                for si in range(2):
                                    kc = ((w + si) % 4) * 256
                                    m_s = mm(
                                        PS[head][:, si * 256:si * 256 + 256],
                                        qbp(p, qc, P)[64 * head:
                                                      64 * head + 64, :],
                                        krp(p, kc, 256)[64 * head:
                                                        64 * head + 64, :],
                                        start=True, stop=True,
                                        mark=(si == 1))
                                vq.wait(m_s)
                            vq.wait(m_scb_user[0])
                            for head in range(2):
                                m_sc = stt(vq, SCB[:, head * 512:
                                                   head * 512 + 512],
                                           PS[head][:, 0:512],
                                           SMALL[:, 176:177],
                                           p2(mbase + qh * 512, 512),
                                           ALU.mult, ALU.add,
                                           mark=(head == 1))
                            bank_free[0] = m_sc
                            bank_free[1] = m_sc
                            for head in range(2):
                                m_nm = redu(vq, SMALL[:, 168 + head:
                                                      169 + head],
                                            SCB[:, head * 512:
                                                head * 512 + 512],
                                            ALU.max, negate=True, mark=True)
                            sq.wait(m_nm)
                            sq.wait(m_pex_user[0])
                            for head in range(2):
                                m_ex = act(sq, p2(PEX0 + head * 512, 512),
                                           SCB[:, head * 512:
                                               head * 512 + 512],
                                           AF.Exp,
                                           bias=SMALL[:, 168 + head:
                                                      169 + head],
                                           mark=(head == 1))
                            m_scb_user[0] = m_ex
                            vq.wait(m_ex)
                            for head in range(2):
                                m_dn = redu(vq, SMALL[:, 170 + head:
                                                      171 + head],
                                            p2(PEX0 + head * 512, 512),
                                            ALU.add, mark=True)
                            sq.wait(m_dn)
                            act(sq, SMALL[:, 172:174], SMALL[:, 170:172],
                                AF.Ln)
                            m_rc = act(sq, SMALL[:, 172:174],
                                       SMALL[:, 172:174], AF.Exp,
                                       scale=-1.0, mark=True)
                            vq.wait(m_rc)
                            for head in range(2):
                                m_pn = ts(vq, p2(PEX0 + head * 512, 512),
                                          p2(PEX0 + head * 512, 512),
                                          SMALL[:, 172 + head:173 + head],
                                          None, ALU.mult, ALU.add,
                                          mark=(head == 1))
                            for head in range(2):
                                tq.wait(m_pn)
                                tq.wait(bank_free.get("tb"))
                                for ck in range(4):
                                    m_tr = tq.run(
                                        lambda e, head=head, ck=ck:
                                        e.transpose(
                                            PSTB[:, ck * P:(ck + 1) * P],
                                            PBF2[:, PEX0 + head * 512 +
                                                 ck * P:PEX0 + head * 512 +
                                                 (ck + 1) * P],
                                            IDENTB[:]), mark=(ck == 3))
                                m_pex_user[0] = m_tr
                                sq.wait(m_tr)
                                sq.wait(m_pts_user[0])
                                m_pt = act(sq, p2(PTS0, 512),
                                           PSTB[:, 0:512], AF.Copy,
                                           mark=True)
                                bank_free["tb"] = m_pt
                                tq.wait(m_pt)
                                tq.wait(bank_free.get(2))
                                for ck in range(4):
                                    m_av = mm(
                                        PS[2][:, head * P:head * P + P],
                                        vrb(2 * w + ck)[:, p * P:
                                                        (p + 1) * P],
                                        p2(PTS0 + ck * P, P),
                                        start=(ck == 0), stop=(ck == 3),
                                        mark=(ck == 3))
                                m_pts_user[0] = m_av
                                sq.wait(m_av)
                                sq.wait(m_wo_user)
                                m_ao = act(
                                    sq,
                                    PBF2[64 * head:64 * head + 64,
                                         AOB0 + p * 512 + aoc + qh * P:
                                         AOB0 + p * 512 + aoc + qh * P + P],
                                    PS[2][64 * head:64 * head + 64,
                                          head * P:head * P + P],
                                    AF.Copy, mark=True)
                            bank_free[2] = m_ao
                            m_ao_last = m_ao

                m_pex_user = [None]
                m_pts_user = [None]
                m_scb_user = [None]
                if mc == 0:
                    fire(0)
                else:
                    if 2 * mc - 1 <= 7:
                        fire(2 * mc - 1)
                    j = mc - 1
                    tq.wait(m_ao_last)
                    for mg in range(G):
                        slot, m_w = wstream(
                            D["wo_b"][lr:lr + C, mg * P:(mg + 1) * P], 0, 0)
                        tq.wait(m_w)
                        tq.wait(bank_free.get(0))
                        for kg in range(G):
                            m_wo = mm(PS[0][:, 0:512],
                                      pb(slot + kg * P, P),
                                      p2(AOB0 + kg * 512, 512),
                                      start=(kg == 0), stop=(kg == G - 1),
                                      mark=(kg == G - 1))
                        ws_state[slot] = m_wo
                        vq.wait(m_wo)
                        m_res = tt(vq, xsg(mg, OWN0 + 512 * j, 512),
                                   PS[0][:, 0:512],
                                   xsg(mg, OWN0 + 512 * j, 512), ALU.add,
                                   mark=True)
                        bank_free[0] = m_res
                    m_wo_user = m_wo
                    if 2 * mc <= 7:
                        fire(2 * mc)
                cx.barrier()

            for g in range(G):
                m_st = cpy(vq, vcols(152 + g, 1), xsg(g, 2305, 1), mark=True)
            yq.wait(m_st)
            m_bc1 = [yq.dma(gwkv_in.ap()[:, 0:8], vcols(152, 8))]
            pq.wait_all(m_bc1)
            m_cc1 = cx.collective("AllGather", ALU.bypass,
                                  [list(range(NCORES))],
                                  [gwkv_in.ap().opt()],
                                  [gwkv_out.ap().opt()])
            yq.wait(m_cc1)
            m_z = memset(vq, vcols(160, 8), 0.0, mark=True)
            for r in range(NCORES):
                m_gl = yq.dma(vcols(168, 8),
                              gwkv_out.ap()[r * P:(r + 1) * P, 0:8])
                vq.wait(m_gl)
                vq.wait(m_z)
                m_acc = stt(vq, vcols(160, 8), vcols(168, 8),
                            SMALL[:, 72 + r:73 + r], vcols(160, 8),
                            ALU.mult, ALU.add, mark=True)
                yq.wait(m_acc)
            for g in range(G):
                cpy(vq, xsg(g, 257, 1), vcols(160, 8)[:, g:g + 1])
            cx.barrier()

            m_gtt_user = None
            m_xnb_user = None
            for t4 in (3, 2, 1, 0):
                c0f = OWN0 + 512 * t4

                def ffsrc(g, c0f=c0f):
                    return xsg(g, c0f - (1 if g >= 4 else 0), 512)

                m_bc = fm_ln_stats(ffsrc, 512)
                vq.wait(m_bc)
                vq.wait(m_xnb_user)
                for g in range(G):
                    tt(vq, FTMP, ffsrc(g), MBC[:, 0:512], ALU.subtract)
                    tt(vq, FTMP, FTMP, RBC[:, 0:512], ALU.mult)
                    m_xnb = ts(vq, pb(XNB0 + g * 512, 512), FTMP,
                               LNGC[:, g:g + 1], LNBC[:, g:g + 1],
                               ALU.mult, ALU.add, mark=True)
                tq.wait(m_xnb)
                for fc in range(FFC):
                    pq.wait(ws_state[W1A0])
                    m_wa = pq.dma(
                        pb(W1A0, 1024).rearrange("p (kg m) -> p kg m", m=P),
                        D["w1a"][lr:lr + C, fc * P:(fc + 1) * P]
                        .rearrange("(kg p) m -> p kg m", p=P))
                    pq.wait(ws_state[W1G0])
                    m_wg = pq.dma(
                        pb(W1G0, 1024).rearrange("p (kg m) -> p kg m", m=P),
                        D["w1g"][lr:lr + C, fc * P:(fc + 1) * P]
                        .rearrange("(kg p) m -> p kg m", p=P))
                    tq.wait(m_wa)
                    tq.wait(m_wg)
                    tq.wait(bank_free.get(0))
                    tq.wait(bank_free.get(1))
                    for kg in range(G):
                        m_a = mm(PS[0][:, 0:512], pb(W1A0 + kg * P, P),
                                 pb(XNB0 + kg * 512, 512),
                                 start=(kg == 0), stop=(kg == G - 1),
                                 mark=(kg == G - 1))
                    ws_state[W1A0] = m_a
                    for kg in range(G):
                        m_gm = mm(PS[1][:, 0:512], pb(W1G0 + kg * P, P),
                                  pb(XNB0 + kg * 512, 512),
                                  start=(kg == 0), stop=(kg == G - 1),
                                  mark=(kg == G - 1))
                    ws_state[W1G0] = m_gm
                    sq.wait(m_gm)
                    sq.wait(m_gtt_user)
                    m_g = act(sq, GTT, PS[1][:, 0:512], AF.Gelu,
                              bias=B1G[:, fc:fc + 1], mark=True)
                    bank_free[1] = m_g
                    vq.wait(m_g)
                    vq.wait(m_a)
                    m_hh = stt(vq, pb(HH0 + fc * 512, 512), PS[0][:, 0:512],
                               B1A[:, fc:fc + 1], GTT, ALU.add, ALU.mult,
                               mark=True)
                    m_gtt_user = m_hh
                    bank_free[0] = m_hh
                tq.wait(m_hh)
                for mg in range(G):
                    pq.wait(ws_state["w2"])
                    m_w2 = pq.dma(
                        p2(W2S0, 2816).rearrange("p (fc m) -> p fc m", m=P),
                        D["w2p"][l * FFI:(l + 1) * FFI, mg * P:(mg + 1) * P]
                        .rearrange("(fc p) m -> p fc m", p=P))
                    tq.wait(m_w2)
                    tq.wait(bank_free.get(2))
                    for fc in range(FFC):
                        m_2 = mm(PS[2][:, 0:512], p2(W2S0 + fc * P, P),
                                 pb(HH0 + fc * 512, 512),
                                 start=(fc == 0), stop=(fc == FFC - 1),
                                 mark=(fc == FFC - 1))
                    ws_state["w2"] = m_2
                    vq.wait(m_2)
                    m_res = stt(vq, xsg(mg, c0f, 512), PS[2][:, 0:512],
                                B2C[:, mg:mg + 1], xsg(mg, c0f, 512),
                                ALU.add, ALU.add, mark=True)
                    bank_free[2] = m_res
                m_xnb_user = tq.last()
                cx.barrier()

        if dump_xs:
            dbg_bxs = nc.dram_tensor("dbg_bxs", [C, EX], F32,
                                     kind="ExternalOutput").ap()
            for g in range(G):
                yq.dma(dbg_bxs[g * P:(g + 1) * P, :],
                       XS[:, g * EX:(g + 1) * EX])
            yq.wait(yq.last_dma())

    TMPQ2 = PF2[:, 0:512]
    if stage == "A":
        dbg_s = nc.dram_tensor("dbg_s", [C, T], F32, kind="ExternalOutput").ap()
        dbg_y = nc.dram_tensor("dbg_y", [C, T + 8], F32,
                               kind="ExternalOutput").ap()
        dbg_vec = nc.dram_tensor("dbg_vec", [P, 184], F32,
                                 kind="ExternalOutput").ap()
    
        dbg_xsfull = nc.dram_tensor("dbg_xsfull", [C, EX], F32,
                                    kind="ExternalOutput").ap()
        for g in range(G):
            yq.dma(dbg_xsfull[g * P:(g + 1) * P, :],
                   XS[:, g * EX:(g + 1) * EX])
        dbg_xa = nc.dram_tensor("dbg_xa", [C, T + 8], F32,
                                kind="ExternalOutput").ap()
        dbg_rwb = nc.dram_tensor("dbg_rwb", [P, 2064], F32,
                                 kind="ExternalOutput").ap()
        dbg_wot = nc.dram_tensor("dbg_wot", [P, 8192], F32,
                                 kind="ExternalOutput").ap()
        RWBF = nc.dram_tensor("rwbf32", [P, 2064], F32).ap()
        WOTF = nc.dram_tensor("wotf32", [P, 8192], F32).ap()
        cpy(vq, PF1[:, 0:2064], PBF[:, 12288:14352])
        m_c1 = cpy(vq, PF1[:, 2064:4112][:, 0:2048],
                   PBF[:, 0:8192][:, 0:2048], mark=True)
        yq.wait(m_c1)
        yq.dma(RWBF, PF1[:, 0:2064])
        yq.dma(dbg_rwb, RWBF)
        yq.dma(WOTF[:, 0:2048], PF1[:, 2064:4112][:, 0:2048])
        cpy(vq, PF1[:, 2064:4112][:, 0:2048], PBF[:, 2048:4096], mark=True)
        yq.wait(vq.last())
        yq.dma(WOTF[:, 2048:4096], PF1[:, 2064:4112][:, 0:2048])
        cpy(vq, PF1[:, 2064:4112][:, 0:2048], PBF[:, 4096:6144], mark=True)
        yq.wait(vq.last())
        yq.dma(WOTF[:, 4096:6144], PF1[:, 2064:4112][:, 0:2048])
        cpy(vq, PF1[:, 2064:4112][:, 0:2048], PBF[:, 6144:8192], mark=True)
        yq.wait(vq.last())
        yq.dma(WOTF[:, 6144:8192], PF1[:, 2064:4112][:, 0:2048])
        yq.dma(dbg_wot, WOTF)
        yq.dma(dbg_xa, xaspill)
        dbg_sr = nc.dram_tensor("dbg_sr", [C, T], F32,
                                kind="ExternalOutput").ap()
        yq.dma(dbg_sr, srspill)
        yq.dma(dbg_y, yspill)
        yq.dma(dbg_vec, VEC[:, 0:184])
        m_fin = yq.dma(dbg_s, sspill)
        yq.wait(m_fin)
        m_f2 = yq.dma(out, D["x_own"])
        yq.wait(m_f2)
    else:
        PROD = PF1[:, 0:4096]
        OUTTM = PF1[:, 4096:5120]
        XTM2 = PF1[:, 5120:6144]
        SSL = PF1[:, 6144:6656]
        m_pr_users = None
        m_outtm_users = None
        for t4 in range(4):
            cb = OWN0 + 512 * t4
            m_pr = None
            for g in range(G):
                m_sl = yq.dma(SSL, sspill[g * P:(g + 1) * P,
                                          512 * t4:512 * t4 + 512])
                sq.wait(m_pr)
                m_sil = act(sq, TMPQ2[:, 0:512], xsg(g, cb, 512), AF.Silu,
                            mark=True)
                vq.wait(m_sl)
                vq.wait(m_sil)
                vq.wait(m_pr_users)
                m_pr = tt(vq, PROD[:, g * 512:(g + 1) * 512], TMPQ2[:, 0:512],
                          SSL, ALU.mult, mark=True)
                yq.wait(m_pr)
            tq.wait(m_pr)
            for tb in range(4):
                m_xl = yq.dma(XTM2, D["x_own"][512 * t4 + 128 * tb:
                                              512 * t4 + 128 * tb + 128, :])
                for g in range(G):
                    bank = PS[g % 2]
                    tq.wait(bank_free.get(g % 2))
                    m_tr = tq.run(lambda e, g=g, tb=tb, bank=bank:
                                  e.transpose(bank[:, 0:P],
                                              PROD[:, g * 512 + tb * P:
                                                   g * 512 + tb * P + P],
                                              IDENT[:]), mark=True)
                    sq.wait(m_tr)
                    sq.wait(m_outtm_users)
                    m_ev = act(sq, OUTTM[:, g * P:(g + 1) * P], bank[:, 0:P],
                               AF.Copy, mark=True)
                    bank_free[g % 2] = m_ev
                vq.wait(m_ev)
                vq.wait(m_xl)
                m_add = tt(vq, OUTTM[:], OUTTM[:], XTM2[:], ALU.add, mark=True)
                yq.wait(m_add)
                m_st = yq.dma(out[512 * t4 + 128 * tb:512 * t4 + 128 * tb + 128,
                                  :], OUTTM[:])
                m_outtm_users = [m_st, vq.last()]
            m_pr_users = tq.last()
        yq.wait(yq.last_dma())

    return emit()


def kernel(**inputs):
    in_maps = host_prep(inputs)
    nc = build_program()
    res = run_bass_kernel_spmd(nc, in_maps, core_ids=list(range(NCORES)))
    outs = [r["out"] for r in res.results]
    return np.stack([np.concatenate(outs[0:4], 0),
                     np.concatenate(outs[4:8], 0)], 0)
